# version 05 (final)
# baseline (speedup 1.0000x reference)
import contextlib
import math
import os
import numpy as np
import ml_dtypes
import concourse.bass as bass
import concourse.mybir as mybir
from concourse.bass_utils import run_bass_kernel_spmd

F32 = mybir.dt.float32
BF16 = mybir.dt.bfloat16
AF = mybir.ActivationFunctionType
ALU = mybir.AluOpType
BF = ml_dtypes.bfloat16
AX = mybir.AxisListType.X

NC = 8
T = 1024
E = 2048
KT = 16
H = 16
DH = 128
NE = 8
FG = 2048
FB = 8192
HG = 2
NGR = H // HG
AHG = 2
EPS = 1e-5
SCALE = 1.0 / math.sqrt(DH)
GC = 1.0 / math.e
GL = 0.5 - math.log(4.0 / 3.0)
GU = 0.5
LAM = [1.0 - 2.0 ** (-5.0 - h) for h in range(H)]
C_G = math.sqrt(2.0 / math.pi)


class Builder:
    def __init__(self):
        self.ops = []
        self.ctr = {k: 0 for k in ("SQ", "GP", "PE", "ACT", "DVE", "CC")}
        for q in ("SQ", "GP"):
            for i in range(8):
                self.ctr["%s%d" % (q, i)] = 0
        self.dma_idx = {"SQ": 0, "GP": 0}

    def _add(self, eng, fn, deps, sem, amt):
        deps = [d for d in deps if d is not None]
        tok = None
        incs = []
        if amt:
            self.ctr[sem] += amt
            tok = (sem, self.ctr[sem])
            incs = [(sem, amt)]
        self.ops.append((eng, deps, fn, incs))
        return tok

    def pe(self, fn, deps=(), inc=1):
        if not os.environ.get("KNOINC"):
            inc = 1
        return self._add("PE", fn, deps, "PE", 1 if inc else 0)

    def act(self, fn, deps=()):
        deps = list(deps) + ([("ACT", self.ctr["ACT"])] if self.ctr["ACT"] else [])
        return self._add("ACT", fn, deps, "ACT", 1)

    def dve(self, fn, deps=()):
        deps = list(deps) + ([("DVE", self.ctr["DVE"])] if self.ctr["DVE"] else [])
        return self._add("DVE", fn, deps, "DVE", 1)

    def pool(self, fn, deps=()):
        return self._add("GP", fn, deps, "GP", 1)

    def dma(self, q, fn, deps=()):
        idx = self.dma_idx[q]
        self.dma_idx[q] = idx + 1
        sem = "%s%d" % (q, idx % 8)
        deps = list(deps) + [(sem, self.ctr[sem])] if self.ctr[sem] > 0 else list(deps)
        return self._add(q, fn, deps, sem, 16)

    def cc(self, fn, deps=()):
        self.ctr["CC"] += 1
        n = self.ctr["CC"] - 1
        self.ops.append(("GP", [d for d in deps if d is not None],
                         lambda eng, _fn=fn: _fn(eng), [("CCx", n)]))
        return ("CC%d" % n, 1)

    def fence(self, sems=None):
        if sems is None:
            sems = ["PE", "ACT", "DVE", "GP"] + ["SQ%d" % i for i in range(8)] + ["GP%d" % i for i in range(8)]
        return [(s, self.ctr[s]) for s in sems if self.ctr[s] > 0]

    def emit(self, block, semmap):
        engmap = {"SQ": block.sync, "GP": block.gpsimd, "PE": block.tensor,
                  "ACT": block.scalar, "DVE": block.vector}

        def make(engname):
            def run(eng):
                water = {}
                for (oeng, deps, fn, incs) in self.ops:
                    if oeng != engname:
                        continue
                    for (sem, val) in deps:
                        if water.get(sem, 0) >= val:
                            continue
                        eng.wait_ge(semmap[sem], val)
                        water[sem] = val
                    ins = fn(eng)
                    for (sem, amt) in incs:
                        if sem == "CCx":
                            ins.then_inc(semmap["CC%d" % amt])
                        else:
                            ins.then_inc(semmap[sem], amt)
            return run

        for name in ("SQ", "GP", "DVE", "ACT", "PE"):
            engmap[name](make(name))


def build_nc():
    nc = bass.Bass("TRN2", target_bir_lowering=False)
    B = Builder()

    def din(name, shape, dtype=F32):
        return nc.dram_tensor(name, list(shape), dtype, kind="ExternalInput").ap()

    xin = din("xin", (T, E))
    Wq = din("Wq", (E, E)); bq = din("bq", (128, KT))
    Wk = din("Wk", (E, DH)); bk = din("bk", (128, 1))
    Wv = din("Wv", (E, DH)); bv = din("bv", (1, DH))
    Wo = din("Wo", (E, E)); bo = din("bo", (128, KT))
    qw = din("qw", (E, E)); kw = din("kw", (E, E))
    vw = din("vw", (E, E)); gw = din("gw", (E, E))
    ow = din("ow", (E, E))
    fc1w = din("fc1w", (E, FB)); fc1b = din("fc1b", (128, FB // 128))
    fc2w = din("fc2w", (FB, E)); fc2b = din("fc2b", (128, KT))
    retnw = din("retnw", (128, KT)); finalw = din("finalw", (128, KT))
    Wg = din("Wg", (NE, E, FG)); Wu = din("Wu", (NE, E, FG))
    Wd = din("Wd", (NE, FG, E)); Wr = din("Wr", (E, NE))
    temp = din("temp", (1, 1))
    IDF = din("IDF", (128, 128)); IDB = din("IDB", (128, 128), BF16)
    ONES1 = din("ONES1", (1, 128))
    ONESC = din("ONESC", (128, 1), BF16)
    P2T = din("P2T", (128, 128), BF16); PRT = din("PRT", (128, 128), BF16)
    RQC = din("RQC", (128, T), BF16); RQS = din("RQS", (128, T), BF16)
    RKC = din("RKC", (128, T), BF16); RKS = din("RKS", (128, T), BF16)
    TQC = din("TQC", (128, T), BF16); TQS = din("TQS", (128, T), BF16)
    TKC = din("TKC", (128, T), BF16); TKS = din("TKS", (128, T), BF16)
    DTm = din("DTm", (128, H * 128), BF16)
    QLAM = din("QLAM", (128, H * 128), BF16)
    LCOL = din("LCOL", (128, H))
    RROW = din("RROW", (128, H * 8))
    CDEC = din("CDEC", (128, NC * H))
    OH8 = din("OH8", (8, NE * 128), BF16)

    yout = nc.dram_tensor("yout", [T, E], F32, kind="ExternalOutput").ap()

    KSRC = nc.dram_tensor("KSRC", [128, T], BF16, kind="Internal").ap()
    VSRC = nc.dram_tensor("VSRC", [T, 128], BF16, kind="Internal").ap()
    KDST = nc.dram_tensor("KDST", [128 * NC, T], BF16, kind="Internal").ap()
    VDST = nc.dram_tensor("VDST", [T * NC, 128], BF16, kind="Internal").ap()
    SSRC = [nc.dram_tensor("SSRC%d" % g, [128, HG * 132], BF16, kind="Internal").ap() for g in range(NGR)]
    SDST = [nc.dram_tensor("SDST%d" % g, [128 * NC, HG * 132], BF16, kind="Internal").ap() for g in range(NGR)]
    XSPILL = nc.dram_tensor("XSPILL", [128, KT * T], F32, kind="Internal").ap()

    with contextlib.ExitStack() as es:
        sbuf = lambda name, shape, dt=F32: es.enter_context(nc.sbuf_tensor(name, list(shape), dt))
        XT = sbuf("XT", (128, KT * T))
        HT = sbuf("HT", (128, KT * T), BF16)
        AB = sbuf("AB", (128, 15 * 1024), BF16)
        cI = sbuf("cI", (128, 128)); cIB = sbuf("cIB", (128, 128), BF16)
        c1 = sbuf("c1", (1, 128)); cC = sbuf("cC", (128, 1), BF16)
        cP2 = sbuf("cP2", (128, 128), BF16); cPR = sbuf("cPR", (128, 128), BF16)
        cTA = sbuf("cTA", (128, T), BF16); cTB = sbuf("cTB", (128, T), BF16)
        cTC = sbuf("cTC", (128, T), BF16); cTD = sbuf("cTD", (128, T), BF16)
        cDT = sbuf("cDT", (128, H * 128), BF16)
        cQL = sbuf("cQL", (128, H * 128), BF16)
        cLC = sbuf("cLC", (128, H))
        cRR = sbuf("cRR", (128, H * 8))
        cCD = sbuf("cCD", (128, NC * H))
        cOH = sbuf("cOH", (8, NE * 128), BF16)
        cbq = sbuf("cbq", (128, KT)); cbk = sbuf("cbk", (128, 1))
        cbvr = sbuf("cbvr", (1, 128)); cbvb = sbuf("cbvb", (128, 128))
        cbo = sbuf("cbo", (128, KT))
        cfc1b = sbuf("cfc1b", (128, FB // 128)); cfc2b = sbuf("cfc2b", (128, KT))
        cretn = sbuf("cretn", (128, KT)); cfinal = sbuf("cfinal", (128, KT))
        ctmp = sbuf("ctmp", (1, 1)); cRCOL = sbuf("cRCOL", (128, 1))
        ROWA = sbuf("ROWA", (1, 512)); ROWB = sbuf("ROWB", (1, 512)); ROWC = sbuf("ROWC", (1, 512))
        RBF = sbuf("RBF", (128, 512))
        TS1 = sbuf("TS1", (128, 512)); TS2 = sbuf("TS2", (128, 512))
        XBS = sbuf("XBS", (128, 512), BF16); SQS = sbuf("SQS", (128, 512), BF16)
        WS = [sbuf("WS%d" % i, (128, 16 * 128), BF16) for i in range(6)]
        WV2 = [sbuf("WV2%d" % i, (128, 512), BF16) for i in range(2)]
        WRB = sbuf("WRB", (128, KT * 8), BF16)
        W8 = sbuf("W8", (128, 64)); WT8 = sbuf("WT8", (8, T), BF16)
        WBC = sbuf("WBC", (128, 512), BF16)
        RF = sbuf("RF", (128, HG * 132))
        RA = sbuf("RA", (128, HG * 132), BF16)
        PS = [es.enter_context(nc.psum_tensor("PS%d" % i, [128, 512], F32)) for i in range(7)]
        PSB = es.enter_context(nc.psum_tensor("PSB", [128, 1024], BF16))

        sems = {}
        for name in ("SQ", "GP", "PE", "ACT", "DVE"):
            sems[name] = es.enter_context(nc.semaphore("s" + name))
        for q in ("SQ", "GP"):
            for i in range(8):
                sems["%s%d" % (q, i)] = es.enter_context(nc.semaphore("s%s%d" % (q, i)))
        NCC = 2 + NGR
        for i in range(NCC):
            sems["CC%d" % i] = es.enter_context(nc.semaphore("sCC%d" % i))
        block = es.enter_context(nc.Block())

        def ar(col_kb, ncols):
            s = col_kb * 512
            return AB[:, s:s + ncols]

        consts1 = [
            (cI, IDF), (cIB, IDB), (c1, ONES1), (cC, ONESC), (cP2, P2T), (cPR, PRT),
            (cTA, RQC), (cTB, RQS), (cTC, RKC), (cTD, RKS),
            (cDT, DTm), (cQL, QLAM), (cLC, LCOL), (cRR, RROW), (cCD, CDEC), (cOH, OH8),
            (cbq, bq), (cbk, bk), (cbvr, bv), (cbo, bo), (cfc1b, fc1b), (cfc2b, fc2b),
            (cretn, retnw), (cfinal, finalw), (ctmp, temp),
        ]
        ctoks = []
        for (dst, src) in consts1:
            ctoks.append(B.dma("SQ", lambda e, d=dst, s=src: e.dma_start(out=d[:], in_=s[:])))
        callc = B.fence(["SQ%d" % i for i in range(8)])

        t1 = B.dve(lambda e: e.reciprocal(out=ctmp[:], in_=ctmp[:]), deps=callc)
        t2 = B.pe(lambda e: e.matmul(out=PS[6][:, 0:1], lhsT=c1[:], rhs=ctmp[:], start=True, stop=True), deps=[t1] + callc)
        t3 = B.dve(lambda e: e.tensor_copy(out=cRCOL[:], in_=PS[6][:, 0:1]), deps=[t2])
        t4 = B.pe(lambda e: e.matmul(out=PS[6][:, 0:128], lhsT=c1[:], rhs=cbvr[:], start=True, stop=True), deps=callc + [t3])
        t5 = B.dve(lambda e: e.tensor_copy(out=cbvb[:], in_=PS[6][:, 0:128]), deps=[t4])

        prev_cp = {0: None, 1: None}
        prev_ld = {0: None, 1: None}
        for tt in range(8):
            for q4 in range(4):
                stg = TS1 if (q4 % 2 == 0) else TS2
                sl = q4 % 2
                ld = B.dma("SQ", lambda e, tt=tt, q4=q4, stg=stg: e.dma_start(
                    out=stg[:], in_=xin[tt * 128:(tt + 1) * 128, q4 * 512:(q4 + 1) * 512]),
                    deps=[prev_cp[sl]] if prev_cp[sl] else callc)
                lcp = None
                for j in range(4):
                    kt = q4 * 4 + j
                    bank = kt % 2
                    tr = B.pe(lambda e, stg=stg, j=j, bank=bank: e.transpose(
                        out=PS[bank][:, 0:128], in_=stg[:, j * 128:(j + 1) * 128], identity=cI[:]),
                        deps=[ld, ctoks[0]] + ([lcp] if lcp else []))
                    lcp = B.dve(lambda e, kt=kt, tt=tt, bank=bank: e.tensor_copy(
                        out=XT[:, kt * T + tt * 128: kt * T + tt * 128 + 128], in_=PS[bank][:, 0:128]),
                        deps=[tr])
                prev_cp[sl] = lcp
        xdone = B.fence()

        def emit_norm(kind, wcol=None, out_f32=False, deps=()):
            deps = list(deps)
            lastc = None
            for tb in range(2):
                pm2 = None
                m1 = None
                for kt in range(KT):
                    c0 = kt * T + tb * 512
                    cast = B.dve(lambda e, c0=c0: e.tensor_copy(out=XBS[:], in_=XT[:, c0:c0 + 512]),
                                 deps=deps + ([pm2] if pm2 else []) + ([lastc] if lastc else []))
                    sq = B.act(lambda e: e.activation(out=SQS[:], in_=XBS[:], func=AF.Square),
                               deps=[cast] + ([pm2] if pm2 else []))
                    if kind == "ln":
                        m1 = B.pe(lambda e, kt=kt: e.matmul(out=PS[5][0:1, 0:512], lhsT=cC[:], rhs=XBS[:],
                                                            start=(kt == 0), stop=(kt == KT - 1), skip_group_check=True),
                                  deps=[cast])
                    m2 = B.pe(lambda e, kt=kt: e.matmul(out=PS[6][0:1, 0:512], lhsT=cC[:], rhs=SQS[:],
                                                        start=(kt == 0), stop=(kt == KT - 1), skip_group_check=True),
                              deps=[sq])
                    pm2 = m2
                if kind == "ln":
                    a1 = B.dve(lambda e: e.tensor_scalar_mul(out=ROWA[:], in0=PS[5][0:1, 0:512], scalar1=1.0 / E), deps=[m1, pm2])
                    a2 = B.act(lambda e: e.activation(out=ROWC[:], in_=ROWA[:], func=AF.Square), deps=[a1])
                    a3 = B.dve(lambda e: e.tensor_scalar_mul(out=ROWB[:], in0=PS[6][0:1, 0:512], scalar1=1.0 / E), deps=[pm2, a2])
                    a4 = B.dve(lambda e: e.tensor_sub(out=ROWB[:], in0=ROWB[:], in1=ROWC[:]), deps=[a3])
                    a4 = B.dve(lambda e: e.tensor_scalar_add(out=ROWB[:], in0=ROWB[:], scalar1=EPS), deps=[a4])
                    a5 = B.act(lambda e: e.activation(out=ROWB[:], in_=ROWB[:], func=AF.Sqrt), deps=[a4])
                    a6 = B.dve(lambda e: e.reciprocal(out=ROWB[:], in_=ROWB[:]), deps=[a5])
                    b1 = B.pe(lambda e: e.matmul(out=PS[4][:, 0:512], lhsT=c1[:], rhs=ROWA[:], start=True, stop=True),
                              deps=[a6] + ([lastc] if lastc else []))
                    b2 = B.pe(lambda e: e.matmul(out=PS[3][:, 0:512], lhsT=c1[:], rhs=ROWB[:], start=True, stop=True), deps=[a6])
                else:
                    a3 = B.dve(lambda e: e.tensor_scalar(out=ROWB[:], in0=PS[6][0:1, 0:512], scalar1=1.0 / E, scalar2=EPS, op0=ALU.mult, op1=ALU.add), deps=[pm2])
                    a5 = B.act(lambda e: e.activation(out=ROWB[:], in_=ROWB[:], func=AF.Sqrt), deps=[a3])
                    a6 = B.dve(lambda e: e.reciprocal(out=ROWB[:], in_=ROWB[:]), deps=[a5])
                    b2 = B.pe(lambda e: e.matmul(out=PS[3][:, 0:512], lhsT=c1[:], rhs=ROWB[:], start=True, stop=True),
                              deps=[a6] + ([lastc] if lastc else []))
                for kt in range(KT):
                    c0 = kt * T + tb * 512
                    if kind == "ln":
                        d1 = B.dve(lambda e, c0=c0: e.tensor_sub(out=TS1[:], in0=XT[:, c0:c0 + 512], in1=PS[4][:, 0:512]),
                                   deps=[b1, lastc] if lastc else [b1])
                        if out_f32:
                            d2 = B.dve(lambda e, c0=c0: e.tensor_mul(out=XT[:, c0:c0 + 512], in0=TS1[:], in1=PS[3][:, 0:512]), deps=[b2, d1])
                        else:
                            d2 = B.dve(lambda e, c0=c0: e.tensor_mul(out=HT[:, c0:c0 + 512], in0=TS1[:], in1=PS[3][:, 0:512]), deps=[b2, d1])
                    else:
                        d2 = B.dve(lambda e, c0=c0, kt=kt: e.scalar_tensor_tensor(
                            out=HT[:, c0:c0 + 512], in0=XT[:, c0:c0 + 512], scalar=wcol[:, kt:kt + 1],
                            in1=PS[3][:, 0:512], op0=ALU.mult, op1=ALU.mult),
                            deps=[b2] + callc + ([lastc] if lastc else []))
                    lastc = d2
            return B.fence()

        emit_norm("ln", deps=xdone)
        f_ln1 = B.fence()

        KRC = ar(0, 1024)
        VC = ar(2, 1024)
        ldk = B.dma("GP", lambda e: e.dma_start(
            out=WS[0][:, 0:KT * 128], in_=Wk[:, :].rearrange("(k p) m -> p k m", p=128)), deps=f_ln1)
        kprev = None
        for tb in range(2):
            mmt = None
            for k in range(KT):
                mmt = B.pe(lambda e, k=k, tb=tb: e.matmul(
                    out=PS[0][:, 0:512], lhsT=WS[0][:, k * 128:(k + 1) * 128],
                    rhs=HT[:, k * T + tb * 512: k * T + tb * 512 + 512],
                    start=(k == 0), stop=(k == KT - 1), skip_group_check=True),
                    deps=[ldk] + f_ln1 + ([kprev] if kprev else []), inc=(k == KT - 1))
            e1 = B.dve(lambda e: e.tensor_scalar_add(out=TS1[:], in0=PS[0][:, 0:512], scalar1=cbk[:, 0:1]), deps=[mmt])
            e2 = B.dve(lambda e: e.tensor_copy(out=XBS[:], in_=TS1[:]), deps=[e1])
            r1 = B.pe(lambda e: e.matmul(out=PS[1][:, 0:512], lhsT=cP2[:], rhs=XBS[:], start=True, stop=True),
                      deps=[e2] + ([kprev] if kprev else []))
            r2 = B.dve(lambda e, tb=tb: e.tensor_mul(out=TS2[:], in0=TS1[:], in1=cTC[:, tb * 512: tb * 512 + 512]), deps=[e1])
            r3 = B.dve(lambda e, tb=tb: e.tensor_mul(out=TS1[:], in0=PS[1][:, 0:512], in1=cTD[:, tb * 512:tb * 512 + 512]), deps=[r1, r2])
            r4 = B.dve(lambda e, tb=tb: e.tensor_add(out=KRC[:, tb * 512: tb * 512 + 512], in0=TS2[:], in1=TS1[:]), deps=[r3])
            kprev = r4
        kr_done = kprev

        ldv = B.dma("GP", lambda e: e.dma_start(
            out=WS[1][:, 0:KT * 128], in_=Wv[:, :].rearrange("(k p) m -> p k m", p=128)), deps=f_ln1)
        v_toks = []
        for tt in range(8):
            bank = 2 + (tt % 2)
            mmt = None
            for k in range(KT):
                mmt = B.pe(lambda e, k=k, tt=tt, bank=bank: e.matmul(
                    out=PS[bank][:, 0:128], lhsT=HT[:, k * T + tt * 128: k * T + tt * 128 + 128],
                    rhs=WS[1][:, k * 128:(k + 1) * 128],
                    start=(k == 0), stop=(k == KT - 1), skip_group_check=True),
                    deps=[ldv] + f_ln1 + v_toks[-2:-1], inc=(k == KT - 1))
            ev = B.dve(lambda e, tt=tt, bank=bank: e.tensor_add(
                out=VC[:, tt * 128: tt * 128 + 128], in0=PS[bank][:, 0:128], in1=cbvb[:]), deps=[mmt, t5])
            v_toks.append(ev)
        s1 = B.dma("SQ", lambda e: e.dma_start(out=KSRC[:, :], in_=KRC), deps=[kr_done])
        s2 = B.dma("SQ", lambda e: e.dma_start(
            out=VSRC.rearrange("(tt p) d -> p tt d", p=128), in_=VC.rearrange("p (tt d) -> p tt d", d=128)), deps=v_toks[-1:])
        cc_k = B.cc(lambda e: e.collective_compute(
            "AllGather", ALU.bypass, replica_groups=[list(range(NC))],
            ins=[KSRC.opt()], outs=[KDST.opt()]), deps=[s1])
        cc_v = B.cc(lambda e: e.collective_compute(
            "AllGather", ALU.bypass, replica_groups=[list(range(NC))],
            ins=[VSRC.opt()], outs=[VDST.opt()]), deps=[s2])
        cc_kv = cc_v

        QG = AB[:, 0:AHG * 1024]
        AEX = AB[:, 2048:4096]
        CTX = AB[:, 4096:4096 + AHG * 1024]
        KS = [AB[:, 6144 + i * 1024: 7168 + i * 1024] for i in range(3)]
        VS = [AB[:, 9216 + i * 1024: 10240 + i * 1024] for i in range(3)]
        gprev = [cc_kv]
        for hg in range(H // AHG):
            f_g = B.fence()
            qk_prev = {0: None, 1: None}
            rot_prev = None
            for hl in range(AHG):
                h = hg * AHG + hl
                sl = hl % 2
                ldq = B.dma("GP", lambda e, h=h, sl=sl: e.dma_start(
                    out=WS[2 + sl][:, 0:KT * 128].rearrange("p (k m) -> p k m", m=128),
                    in_=Wq[:, h * 128:(h + 1) * 128].rearrange("(k p) m -> p k m", p=128)),
                    deps=f_g + [s1, s2])
                for tb in range(2):
                    c0 = hl * T + tb * 512
                    mmt = None
                    for k in range(KT):
                        mmt = B.pe(lambda e, k=k, tb=tb, sl=sl: e.matmul(
                            out=PS[0 + tb][:, 0:512], lhsT=WS[2 + sl][:, k * 128:(k + 1) * 128],
                            rhs=HT[:, k * T + tb * 512: k * T + tb * 512 + 512],
                            start=(k == 0), stop=(k == KT - 1), skip_group_check=True),
                            deps=[ldq] + f_g + ([qk_prev[tb]] if qk_prev[tb] else []), inc=(k == KT - 1))
                    e1 = B.dve(lambda e, h=h, tb=tb: e.tensor_scalar_add(
                        out=TS1[:], in0=PS[0 + tb][:, 0:512], scalar1=cbq[:, h:h + 1]), deps=[mmt])
                    e2 = B.dve(lambda e: e.tensor_copy(out=XBS[:], in_=TS1[:]),
                               deps=[e1] + ([rot_prev] if rot_prev else []))
                    r1 = B.pe(lambda e: e.matmul(out=PS[2][:, 0:512], lhsT=cP2[:], rhs=XBS[:], start=True, stop=True),
                              deps=[e2] + ([rot_prev] if rot_prev else []))
                    r2 = B.dve(lambda e, tb=tb: e.tensor_mul(out=TS2[:], in0=TS1[:], in1=cTA[:, tb * 512:tb * 512 + 512]), deps=[e1])
                    r3 = B.dve(lambda e, tb=tb: e.tensor_mul(out=TS1[:], in0=PS[2][:, 0:512], in1=cTB[:, tb * 512:tb * 512 + 512]), deps=[r1, r2])
                    r4 = B.dve(lambda e, c0=c0: e.tensor_add(out=QG[:, c0:c0 + 512], in0=TS2[:], in1=TS1[:]), deps=[r3])
                    qk_prev[tb] = r4
                    rot_prev = r3
            f_q = B.fence()
            for hl in range(AHG):
                for sbk in range(2):
                    qcol = hl * T + sbk * 512
                    exp_toks = {}
                    sc_toks = {}
                    kv_toks = {}
                    cs_last = None
                    o_last = None
                    for c8 in range(2):
                        ksl = c8 % 3
                        kv_toks[c8] = [
                            B.dma("SQ", lambda e, c8=c8, ksl=ksl: e.dma_start(
                                out=KS[ksl], in_=KDST[c8 * 128:(c8 + 1) * 128, :]),
                                deps=[cc_k, cc_v] + gprev),
                            B.dma("SQ", lambda e, c8=c8, ksl=ksl: e.dma_start(
                                out=VS[ksl].rearrange("p (tt d) -> p tt d", d=128), in_=VDST[c8 * T:(c8 + 1) * T, :].rearrange("(tt p) d -> p tt d", p=128)),
                                deps=[cc_k, cc_v] + gprev),
                        ]
                    for c8 in range(8):
                        ksl = c8 % 3
                        for tl in range(8):
                            t = c8 * 8 + tl
                            sc = B.pe(lambda e, tl=tl, t=t, ksl=ksl, qcol=qcol: e.matmul(
                                out=PS[t % 2][:, 0:512], lhsT=KS[ksl][:, tl * 128:(tl + 1) * 128],
                                rhs=QG[:, qcol:qcol + 512], start=True, stop=True, skip_group_check=True),
                                deps=list(kv_toks[c8]) + ([exp_toks[t - 2]] if t >= 2 else []) + f_q)
                            sc_toks[t] = sc
                            ex = B.act(lambda e, t=t: e.activation(
                                out=AEX[:, (t % 4) * 512:(t % 4) * 512 + 512], in_=PS[t % 2][:, 0:512], func=AF.Exp),
                                deps=[sc])
                            exp_toks[t] = ex
                            if tl == 1 and c8 + 2 < 8:
                                nsl = (c8 + 2) % 3
                                kv_toks[c8 + 2] = [
                                    B.dma("SQ", lambda e, c8=c8, nsl=nsl: e.dma_start(
                                        out=KS[nsl], in_=KDST[(c8 + 2) * 128:(c8 + 3) * 128, :]),
                                        deps=[sc]),
                                    B.dma("SQ", lambda e, c8=c8, nsl=nsl: e.dma_start(
                                        out=VS[nsl].rearrange("p (tt d) -> p tt d", d=128), in_=VDST[(c8 + 2) * T:(c8 + 3) * T, :].rearrange("(tt p) d -> p tt d", p=128)),
                                        deps=[sc]),
                                ]
                            if t >= 1:
                                cs_last = B.pe(lambda e, t=t: e.matmul(
                                    out=PS[4][0:1, 0:512], lhsT=cC[:], rhs=AEX[:, ((t - 1) % 4) * 512:((t - 1) % 4) * 512 + 512],
                                    start=(t - 1 == 0), stop=False, skip_group_check=True),
                                    deps=[exp_toks[t - 1]])
                                ts_ = t - 1
                                ksl2 = (ts_ // 8) % 3
                                o_last = B.pe(lambda e, ts_=ts_, ksl2=ksl2: e.matmul(
                                    out=PS[3][:, 0:512], lhsT=VS[ksl2][:, (ts_ % 8) * 128:(ts_ % 8) * 128 + 128],
                                    rhs=AEX[:, (ts_ % 4) * 512:(ts_ % 4) * 512 + 512],
                                    start=(ts_ == 0), stop=False, skip_group_check=True),
                                    deps=[exp_toks[ts_]])
                    cs_last = B.pe(lambda e: e.matmul(
                        out=PS[4][0:1, 0:512], lhsT=cC[:], rhs=AEX[:, (63 % 4) * 512:(63 % 4) * 512 + 512],
                        start=False, stop=True, skip_group_check=True), deps=[exp_toks[63]])
                    o_last = B.pe(lambda e: e.matmul(
                        out=PS[3][:, 0:512], lhsT=VS[(63 // 8) % 3][:, (63 % 8) * 128:(63 % 8) * 128 + 128],
                        rhs=AEX[:, (63 % 4) * 512:(63 % 4) * 512 + 512],
                        start=False, stop=True, skip_group_check=True), deps=[exp_toks[63]])
                    rc = B.dve(lambda e: e.reciprocal(out=ROWA[:], in_=PS[4][0:1, 0:512]), deps=[cs_last])
                    bc = B.pe(lambda e: e.matmul(out=PS[5][:, 0:512], lhsT=c1[:], rhs=ROWA[:], start=True, stop=True), deps=[rc])
                    cb = B.dve(lambda e: e.tensor_copy(out=RBF[:], in_=PS[5][:, 0:512]), deps=[bc])
                    ct = B.dve(lambda e, hl=hl, sbk=sbk: e.tensor_mul(
                        out=CTX[:, hl * T + sbk * 512: hl * T + sbk * 512 + 512],
                        in0=PS[3][:, 0:512], in1=RBF[:]), deps=[cb, o_last])
                    gprev = [ct]
            f_fl = B.fence()
            woc = {0: None, 1: None}
            wo_prev = None
            for eo in range(KT):
                sl = eo % 2
                ldw = B.dma("GP", lambda e, eo=eo, hg=hg, sl=sl: e.dma_start(
                    out=WS[4 + sl][:, 0:AHG * 128].rearrange("p (k m) -> p k m", m=128),
                    in_=Wo[hg * AHG * 128:(hg + 1) * AHG * 128, eo * 128:(eo + 1) * 128].rearrange("(k p) m -> p k m", p=128)),
                    deps=f_fl + ([woc[sl]] if woc[sl] else []))
                for tb in range(2):
                    mmt = None
                    for kk in range(AHG):
                        mmt = B.pe(lambda e, kk=kk, tb=tb, sl=sl: e.matmul(
                            out=PS[6][:, 0:512], lhsT=WS[4 + sl][:, kk * 128:(kk + 1) * 128],
                            rhs=CTX[:, kk * T + tb * 512: kk * T + tb * 512 + 512],
                            start=(kk == 0), stop=(kk == AHG - 1), skip_group_check=True),
                            deps=[ldw] + f_fl + ([wo_prev] if wo_prev else []), inc=(kk == AHG - 1))
                    c0 = eo * T + tb * 512
                    if hg == 0:
                        ev = B.dve(lambda e, c0=c0, eo=eo: e.scalar_tensor_tensor(
                            out=XT[:, c0:c0 + 512], in0=PS[6][:, 0:512], scalar=cbo[:, eo:eo + 1],
                            in1=XT[:, c0:c0 + 512], op0=ALU.add, op1=ALU.add), deps=[mmt])
                    else:
                        ev = B.dve(lambda e, c0=c0: e.tensor_add(
                            out=XT[:, c0:c0 + 512], in0=PS[6][:, 0:512], in1=XT[:, c0:c0 + 512]), deps=[mmt])
                    woc[sl] = ev
                    wo_prev = ev
        f_p1 = B.fence()

        def emit_golden(deps):
            emit_norm("ln", deps=deps)
            f0 = B.fence()
            ldr = B.dma("GP", lambda e: e.dma_start(
                out=WRB[:].rearrange("p (k m) -> p k m", m=8), in_=Wr[:, :].rearrange("(k p) m -> p k m", p=128)), deps=f0)
            wprev = None
            trprev = None
            for tt in range(8):
                mmt = None
                for k in range(KT):
                    mmt = B.pe(lambda e, k=k, tt=tt: e.matmul(
                        out=PS[4][:, 0:8], lhsT=HT[:, k * T + tt * 128: k * T + tt * 128 + 128],
                        rhs=WRB[:, k * 8:(k + 1) * 8], start=(k == 0), stop=(k == KT - 1),
                        skip_group_check=True),
                        deps=[ldr] + f0 + ([wprev] if wprev else []), inc=(k == KT - 1))
                inh = B.act(lambda e: e.activation(out=TS1[:, 0:8], in_=PS[4][:, 0:8], func=AF.Sigmoid, scale=cRCOL[:]),
                            deps=[mmt, t3] + ([wprev] if wprev else []))
                dist = B.dve(lambda e: e.tensor_scalar_add(out=TS1[:, 8:16], in0=TS1[:, 0:8], scalar1=-GC), deps=[inh])
                dist = B.act(lambda e: e.activation(out=TS1[:, 8:16], in_=TS1[:, 8:16], func=AF.Abs), deps=[dist])
                z1 = B.dve(lambda e: e.tensor_single_scalar(out=TS1[:, 16:24], in_=TS1[:, 0:8], scalar=GL, op=ALU.is_ge), deps=[dist])
                z2 = B.dve(lambda e: e.tensor_single_scalar(out=TS1[:, 24:32], in_=TS1[:, 0:8], scalar=GU, op=ALU.is_le), deps=[z1])
                zz = B.dve(lambda e: e.tensor_mul(out=TS1[:, 16:24], in0=TS1[:, 16:24], in1=TS1[:, 24:32]), deps=[z2])
                w_ = B.act(lambda e: e.activation(out=TS1[:, 32:40], in_=TS1[:, 8:16], func=AF.Exp, scale=-10.0), deps=[dist, zz])
                w_ = B.dve(lambda e: e.tensor_mul(out=TS1[:, 32:40], in0=TS1[:, 32:40], in1=TS1[:, 16:24]), deps=[w_, zz])
                ws_ = B.dve(lambda e: e.reduce_sum(out=TS2[:, 0:1], in_=TS1[:, 32:40], axis=AX), deps=[w_])
                nx = B.dve(lambda e: e.tensor_single_scalar(out=TS2[:, 1:2], in_=TS2[:, 0:1], scalar=1e-8, op=ALU.is_lt), deps=[ws_])
                fb = B.act(lambda e: e.activation(out=TS1[:, 48:56], in_=TS1[:, 8:16], func=AF.Exp, scale=-10.0 / 3.0), deps=[dist, w_])
                m1_ = B.dve(lambda e: e.reduce_max(out=TS2[:, 2:3], in_=TS1[:, 48:56], axis=AX), deps=[fb])
                mk = B.dve(lambda e: e.tensor_scalar(out=TS1[:, 64:72], in0=TS1[:, 48:56], scalar1=TS2[:, 2:3], scalar2=None, op0=ALU.is_ge), deps=[m1_])
                fbm = B.dve(lambda e: e.scalar_tensor_tensor(out=TS1[:, 72:80], in0=TS1[:, 64:72], scalar=-1e30, in1=TS1[:, 48:56], op0=ALU.mult, op1=ALU.add), deps=[mk])
                m2_ = B.dve(lambda e: e.reduce_max(out=TS2[:, 3:4], in_=TS1[:, 72:80], axis=AX), deps=[fbm])
                t2m = B.dve(lambda e: e.tensor_scalar(out=TS1[:, 80:88], in0=TS1[:, 48:56], scalar1=TS2[:, 3:4], scalar2=None, op0=ALU.is_ge), deps=[m2_])
                fbw = B.dve(lambda e: e.tensor_mul(out=TS1[:, 88:96], in0=TS1[:, 48:56], in1=TS1[:, 80:88]), deps=[t2m])
                fs = B.dve(lambda e: e.reduce_sum(out=TS2[:, 4:5], in_=TS1[:, 88:96], axis=AX), deps=[fbw])
                fs = B.dve(lambda e: e.tensor_scalar_max(out=TS2[:, 4:5], in0=TS2[:, 4:5], scalar1=1e-8), deps=[fs])
                fr = B.dve(lambda e: e.reciprocal(out=TS2[:, 4:5], in_=TS2[:, 4:5]), deps=[fs])
                fbn = B.dve(lambda e: e.tensor_scalar_mul(out=TS1[:, 88:96], in0=TS1[:, 88:96], scalar1=TS2[:, 4:5]), deps=[fr])
                wsc = B.dve(lambda e: e.tensor_scalar_max(out=TS2[:, 5:6], in0=TS2[:, 0:1], scalar1=1e-8), deps=[nx])
                wr = B.dve(lambda e: e.reciprocal(out=TS2[:, 5:6], in_=TS2[:, 5:6]), deps=[wsc])
                wn = B.dve(lambda e: e.tensor_scalar_mul(out=TS1[:, 32:40], in0=TS1[:, 32:40], scalar1=TS2[:, 5:6]), deps=[wr])
                df = B.dve(lambda e: e.tensor_sub(out=TS1[:, 96:104], in0=TS1[:, 88:96], in1=TS1[:, 32:40]), deps=[fbn, wn])
                sel = B.dve(lambda e: e.scalar_tensor_tensor(out=TS1[:, 104:112], in0=TS1[:, 96:104], scalar=TS2[:, 1:2], in1=TS1[:, 32:40], op0=ALU.mult, op1=ALU.add), deps=[df])
                ss = B.dve(lambda e: e.reduce_sum(out=TS2[:, 6:7], in_=TS1[:, 104:112], axis=AX), deps=[sel])
                ss = B.dve(lambda e: e.tensor_scalar_max(out=TS2[:, 6:7], in0=TS2[:, 6:7], scalar1=1e-8), deps=[ss])
                sr = B.dve(lambda e: e.reciprocal(out=TS2[:, 6:7], in_=TS2[:, 6:7]), deps=[ss])
                fin = B.dve(lambda e, tt=tt: e.tensor_scalar_mul(out=W8[:, tt * 8:(tt + 1) * 8], in0=TS1[:, 104:112], scalar1=TS2[:, 6:7]), deps=[sr])
                tr = B.pe(lambda e, tt=tt: e.transpose(out=PS[5][0:8, 0:128], in_=W8[:, tt * 8:(tt + 1) * 8], identity=cI[:]),
                          deps=[fin] + ([trprev] if trprev else []))
                wprev = B.dve(lambda e, tt=tt: e.tensor_copy(out=WT8[0:8, tt * 128:(tt + 1) * 128], in_=PS[5][0:8, 0:128]), deps=[tr])
                trprev = wprev
            f_rt = B.fence()
            HE = ar(0, 8 * 512)
            xg_cons = {}
            for tb in range(2):
                gcons = {0: None, 1: None}
                dcons = {0: None, 1: None}
                wb_prev = None
                he_guard = None
                sg_prev = None
                pr_prev = None
                for ex in range(NE):
                    wbm = B.pe(lambda e, ex=ex, tb=tb: e.matmul(
                        out=PS[4][:, 0:512], lhsT=cOH[:, ex * 128:(ex + 1) * 128],
                        rhs=WT8[0:8, tb * 512: tb * 512 + 512], start=True, stop=True, skip_group_check=True),
                        deps=f_rt + ([wb_prev] if wb_prev else []))
                    wbc = B.dve(lambda e: e.tensor_copy(out=WBC[:], in_=PS[4][:, 0:512]), deps=[wbm])
                    wb_used = []
                    for half in range(2):
                        he_toks = []
                        for fi in range(8):
                            fo = half * 8 + fi
                            slg = fi % 2
                            lg = B.dma("GP", lambda e, ex=ex, fo=fo, slg=slg: e.dma_start(
                                out=WS[slg][:, 0:KT * 128].rearrange("p (k m) -> p k m", m=128),
                                in_=Wg[ex, :, fo * 128:(fo + 1) * 128].rearrange("(k p) m -> p k m", p=128)),
                                deps=[gcons[slg]] if gcons[slg] else f_rt)
                            lu = B.dma("GP", lambda e, ex=ex, fo=fo, slg=slg: e.dma_start(
                                out=WS[2 + slg][:, 0:KT * 128].rearrange("p (k m) -> p k m", m=128),
                                in_=Wu[ex, :, fo * 128:(fo + 1) * 128].rearrange("(k p) m -> p k m", p=128)),
                                deps=[gcons[slg]] if gcons[slg] else f_rt)
                            mg = mu = None
                            for k in range(KT):
                                mg = B.pe(lambda e, k=k, tb=tb, slg=slg: e.matmul(
                                    out=PS[0][:, 0:512], lhsT=WS[slg][:, k * 128:(k + 1) * 128],
                                    rhs=HT[:, k * T + tb * 512: k * T + tb * 512 + 512],
                                    start=(k == 0), stop=(k == KT - 1), skip_group_check=True),
                                    deps=[lg] + ([sg_prev] if sg_prev else []), inc=(k == KT - 1))
                            for k in range(KT):
                                mu = B.pe(lambda e, k=k, tb=tb, slg=slg: e.matmul(
                                    out=PS[1][:, 0:512], lhsT=WS[2 + slg][:, k * 128:(k + 1) * 128],
                                    rhs=HT[:, k * T + tb * 512: k * T + tb * 512 + 512],
                                    start=(k == 0), stop=(k == KT - 1), skip_group_check=True),
                                    deps=[lu] + ([pr_prev] if pr_prev else []), inc=(k == KT - 1))
                            sg = B.act(lambda e: e.activation(out=TS1[:], in_=PS[0][:, 0:512], func=AF.Silu),
                                       deps=[mg] + ([pr_prev] if pr_prev else []))
                            pr = B.dve(lambda e: e.tensor_mul(out=TS2[:], in0=TS1[:], in1=PS[1][:, 0:512]),
                                       deps=[sg, mu])
                            hw = B.dve(lambda e, fi=fi: e.tensor_mul(
                                out=HE[:, fi * 512:(fi + 1) * 512], in0=TS2[:], in1=WBC[:]),
                                deps=[pr, wbc] + ([he_guard] if he_guard else []))
                            sg_prev = sg
                            pr_prev = hw
                            gcons[slg] = hw
                            he_toks.append(hw)
                            wb_used.append(hw)
                        for eo in range(KT):
                            sld = eo % 2
                            ldd = B.dma("GP", lambda e, ex=ex, eo=eo, half=half, sld=sld: e.dma_start(
                                out=WS[4 + sld][:, 0:8 * 128].rearrange("p (k m) -> p k m", m=128),
                                in_=Wd[ex, half * 1024:(half + 1) * 1024, eo * 128:(eo + 1) * 128].rearrange("(k p) m -> p k m", p=128)),
                                deps=[dcons[sld]] if dcons[sld] else f_rt)
                            mmt = None
                            for k in range(8):
                                mmt = B.pe(lambda e, k=k, eo=eo, sld=sld: e.matmul(
                                    out=PS[2 + (eo % 2)][:, 0:512], lhsT=WS[4 + sld][:, k * 128:(k + 1) * 128],
                                    rhs=HE[:, k * 512:(k + 1) * 512],
                                    start=(k == 0), stop=(k == 7), skip_group_check=True),
                                    deps=[ldd, he_toks[k]], inc=(k == 7))
                            c0 = eo * T + tb * 512
                            ev = B.dve(lambda e, c0=c0, eo=eo: e.tensor_add(
                                out=XT[:, c0:c0 + 512], in0=PS[2 + (eo % 2)][:, 0:512], in1=XT[:, c0:c0 + 512]),
                                deps=[mmt] + ([xg_cons.get((tb, eo))] if xg_cons.get((tb, eo)) else []))
                            dcons[sld] = ev
                            xg_cons[(tb, eo)] = ev
                        he_guard = ev
                    wb_prev = wb_used[-1]
            return B.fence()

        f_p2 = emit_golden(f_p1)

        th_toks = []
        for (dst, src) in ((cTA, TQC), (cTB, TQS), (cTC, TKC), (cTD, TKS)):
            th_toks.append(B.dma("SQ", lambda e, d=dst, s=src: e.dma_start(out=d[:], in_=s[:]), deps=f_p2))
        sp = B.dma("SQ", lambda e: e.dma_start(out=XSPILL[:, :], in_=XT[:, :]), deps=f_p2)
        emit_norm("ln", out_f32=True, deps=f_p2 + [sp])
        f_ln3 = B.fence()
        emit_norm("rms", wcol=cretn, deps=f_ln3)
        f_rms = B.fence() + th_toks[-1:]

        QRB = ar(0, HG * 1024)
        KRB = ar(2, HG * 1024)
        VOB = ar(4, HG * 8 * 132)
        SGB = ar(9, HG * 1024)
        REST = ar(11, HG * 1024)
        SGA = ar(13, NC * HG * 132)
        QRB = AB[:, 0:2048]
        KRB = AB[:, 2048:4096]
        VOB = AB[:, 4096:6208]
        SGB = AB[:, 6208:8256]
        REST = AB[:, 8256:10304]
        SGA2 = AB[:, 10304:12416]
        AMS = AB[:, 12416:12544]
        QLS = AB[:, 12544:12672]
        KLS = AB[:, 12672:12800]
        RSN = AB[:, 12800:12928]
        x_ret_cons = {}
        for g in range(NGR):
            f_g0 = B.fence()
            vob_ones = B.dve(lambda e: e.memset(VOB, 1.0), deps=f_g0)
            qk_prev = {0: None, 1: None}
            ws_prev = {0: None, 1: None}
            rot_prev = None
            for hl in range(HG):
                h = g * HG + hl
                for wi, (wd, dst, ctab, stab) in enumerate(
                        ((qw, QRB, cTA, cTB), (kw, KRB, cTC, cTD))):
                    sl = (hl * 2 + wi) % 2
                    ldq = B.dma("GP", lambda e, h=h, sl=sl, wd=wd: e.dma_start(
                        out=WS[2 + sl][:, 0:KT * 128].rearrange("p (k m) -> p k m", m=128),
                        in_=wd[:, h * 128:(h + 1) * 128].rearrange("(k p) m -> p k m", p=128)),
                        deps=f_g0 + ([ws_prev[sl]] if ws_prev[sl] else []))
                    for tb in range(2):
                        mmt = None
                        for k in range(KT):
                            mmt = B.pe(lambda e, k=k, tb=tb, sl=sl: e.matmul(
                                out=PS[0 + tb][:, 0:512], lhsT=WS[2 + sl][:, k * 128:(k + 1) * 128],
                                rhs=HT[:, k * T + tb * 512: k * T + tb * 512 + 512],
                                start=(k == 0), stop=(k == KT - 1), skip_group_check=True),
                                deps=[ldq] + f_rms + ([qk_prev[tb]] if qk_prev[tb] else []), inc=(k == KT - 1))
                        e2 = B.dve(lambda e, tb=tb: e.tensor_copy(out=XBS[:], in_=PS[0 + tb][:, 0:512]),
                                   deps=[mmt] + ([rot_prev] if rot_prev else []))
                        r1 = B.pe(lambda e: e.matmul(out=PS[2][:, 0:512], lhsT=cPR[:], rhs=XBS[:], start=True, stop=True),
                                  deps=[e2] + ([rot_prev] if rot_prev else []))
                        r2 = B.dve(lambda e, tb=tb, ctab=ctab: e.tensor_mul(
                            out=TS2[:], in0=PS[0 + tb][:, 0:512], in1=ctab[:, tb * 512:tb * 512 + 512]), deps=[e2])
                        r3 = B.dve(lambda e, tb=tb, stab=stab: e.tensor_mul(
                            out=TS1[:], in0=PS[2][:, 0:512], in1=stab[:, tb * 512:tb * 512 + 512]), deps=[r1, r2])
                        r4 = B.dve(lambda e, hl=hl, tb=tb, dst=dst: e.tensor_add(
                            out=dst[:, hl * T + tb * 512: hl * T + tb * 512 + 512], in0=TS2[:], in1=TS1[:]), deps=[r3])
                        qk_prev[tb] = r4
                        rot_prev = r3
                    ws_prev[sl] = mmt
            f_qk = B.fence()
            vg_fence = f_qk
            for (wd, isG) in ((vw, False), (gw, True)):
                kcons = {0: None, 1: None}
                pcons = {}
                for half in range(2):
                    for k in range(KT):
                        slk = k % 2
                        lv = B.dma("GP", lambda e, k=k, slk=slk, wd=wd, g=g: e.dma_start(
                            out=WV2[slk][:, 0:HG * 128],
                            in_=wd[k * 128:(k + 1) * 128, g * HG * 128:(g + 1) * HG * 128]),
                            deps=[kcons[slk]] if kcons[slk] else vg_fence)
                        for ti in range(4):
                            tt = half * 4 + ti
                            mmt = B.pe(lambda e, k=k, tt=tt, ti=ti, slk=slk: e.matmul(
                                out=PS[3 + ti][:, 0:HG * 128], lhsT=HT[:, k * T + tt * 128: k * T + tt * 128 + 128],
                                rhs=WV2[slk][:, 0:HG * 128], start=(k == 0), stop=(k == KT - 1), skip_group_check=True),
                                deps=[lv] + pcons.get(ti, []), inc=(k == KT - 1))
                        kcons[slk] = mmt
                        if k == KT - 1:
                            for ti in range(4):
                                pass
                    for ti in range(4):
                        tt = half * 4 + ti
                        if isG:
                            ev = B.act(lambda e, tt=tt, ti=ti: e.activation(
                                out=SGB[:, tt * HG * 128: tt * HG * 128 + HG * 128], in_=PS[3 + ti][:, 0:HG * 128], func=AF.Silu),
                                deps=[kcons[(KT - 1) % 2]])
                        else:
                            ev = None
                            for hl in range(HG):
                                ev = B.dve(lambda e, tt=tt, ti=ti, hl=hl: e.tensor_copy(
                                    out=VOB[:, hl * (8 * 132) + tt * 132: hl * (8 * 132) + tt * 132 + 128],
                                    in_=PS[3 + ti][:, hl * 128:(hl + 1) * 128]),
                                    deps=[kcons[(KT - 1) % 2], vob_ones])
                        pcons[ti] = [ev]
                vg_fence = B.fence()
            f_vg = vg_fence
            sum_toks = []
            klp = None
            for hl in range(HG):
                h = g * HG + hl
                mmt = None
                for tt in range(8):
                    tr = B.pe(lambda e, hl=hl, tt=tt: e.transpose(
                        out=PSB[:, 0:128], in_=KRB[:, hl * T + tt * 128: hl * T + tt * 128 + 128], identity=cIB[:]),
                        deps=f_vg + ([klp] if klp else []))
                    kl = B.dve(lambda e, h=h, tt=tt: e.tensor_scalar(
                        out=KLS[:], in0=PSB[:, 0:128], scalar1=cLC[:, h:h + 1],
                        scalar2=LAM[h] ** ((7 - tt) * 128), op0=ALU.mult, op1=ALU.mult),
                        deps=[tr] + ([mmt] if mmt else []) + ([klp] if klp else []))
                    mmt = B.pe(lambda e, hl=hl, tt=tt: e.matmul(
                        out=PS[5][:, 0:129], lhsT=KLS[:],
                        rhs=VOB[:, hl * (8 * 132) + tt * 132: hl * (8 * 132) + tt * 132 + 129],
                        start=(tt == 0), stop=(tt == 7), skip_group_check=True),
                        deps=[kl], inc=True)
                    klp = mmt
                cp = B.dve(lambda e, hl=hl: e.tensor_copy(
                    out=SGA2[:, hl * 132: hl * 132 + 132][:, 0:129], in_=PS[5][:, 0:129]), deps=[mmt])
                sum_toks.append(cp)
                klp = cp
            ssd = B.dma("SQ", lambda e, g=g: e.dma_start(out=SSRC[g][:, :], in_=SGA2[:, 0:HG * 132]), deps=sum_toks)
            cc_s = B.cc(lambda e, g=g: e.collective_compute(
                "AllGather", ALU.bypass, replica_groups=[list(range(NC))],
                ins=[SSRC[g].opt()], outs=[SDST[g].opt()]), deps=[ssd])
            gs = B.dma("SQ", lambda e, g=g: e.dma_start(
                out=SGA2.rearrange("p (r n) -> p r n", n=HG * 132), in_=SDST[g].rearrange("(r p) n -> p r n", p=128)), deps=[cc_s])
            rtoks = []
            for hl in range(HG):
                h = g * HG + hl
                tk = B.dve(lambda e, hl=hl, h=h: e.tensor_scalar_mul(
                    out=RF[:, hl * 132: hl * 132 + 129],
                    in0=SGA2[:, hl * 132: hl * 132 + 129], scalar1=cCD[:, h: h + 1]), deps=[gs])
                for cp_ in range(1, NC):
                    tk = B.dve(lambda e, hl=hl, h=h, cp_=cp_: e.scalar_tensor_tensor(
                        out=RF[:, hl * 132: hl * 132 + 129],
                        in0=SGA2[:, cp_ * (HG * 132) + hl * 132: cp_ * (HG * 132) + hl * 132 + 129],
                        scalar=cCD[:, cp_ * H + h: cp_ * H + h + 1],
                        in1=RF[:, hl * 132: hl * 132 + 129], op0=ALU.mult, op1=ALU.add), deps=[tk])
                tk = B.act(lambda e, hl=hl: e.activation(
                    out=RA[:, hl * 132: hl * 132 + 129], in_=RF[:, hl * 132: hl * 132 + 129], func=AF.Copy), deps=[tk])
                rtoks.append(tk)
            fin_prev = None
            am_prev = None
            os_prev = None
            up_prev = None
            kl_prev = None
            for st in range(8):
                for hl in range(HG):
                    h = g * HG + hl
                    qc = hl * T + st * 128
                    a1 = B.pe(lambda e, qc=qc: e.matmul(
                        out=PS[0][:, 0:128], lhsT=KRB[:, qc:qc + 128], rhs=QRB[:, qc:qc + 128],
                        start=True, stop=True, skip_group_check=True),
                        deps=f_vg + [rtoks[hl]] + ([am_prev] if am_prev else []))
                    am = B.dve(lambda e, h=h: e.tensor_mul(out=AMS[:], in0=PS[0][:, 0:128], in1=cDT[:, h * 128:(h + 1) * 128]), deps=[a1])
                    ql = B.dve(lambda e, h=h, qc=qc: e.tensor_mul(
                        out=QLS[:], in0=QRB[:, qc:qc + 128], in1=cQL[:, h * 128:(h + 1) * 128]), deps=[a1])
                    tr = B.pe(lambda e, qc=qc: e.transpose(out=PSB[:, 0:128], in_=KRB[:, qc:qc + 128], identity=cIB[:]),
                              deps=[am] + ([kl_prev] if kl_prev else []))
                    kl = B.dve(lambda e, h=h: e.tensor_scalar_mul(
                        out=KLS[:], in0=PSB[:, 0:128], scalar1=cLC[:, h:h + 1]), deps=[tr])
                    o1_ = B.pe(lambda e, hl=hl, st=st: e.matmul(
                        out=PS[1][:, 0:129], lhsT=AMS[:],
                        rhs=VOB[:, hl * (8 * 132) + st * 132: hl * (8 * 132) + st * 132 + 129],
                        start=True, stop=False, skip_group_check=True),
                        deps=[am] + ([os_prev] if os_prev else []))
                    o2_ = B.pe(lambda e, hl=hl: e.matmul(
                        out=PS[1][:, 0:129], lhsT=QLS[:], rhs=RA[:, hl * 132: hl * 132 + 129],
                        start=False, stop=True, skip_group_check=True), deps=[ql, rtoks[hl]])
                    dr = B.pe(lambda e, hl=hl, st=st: e.matmul(
                        out=PS[2][:, 0:129], lhsT=KLS[:],
                        rhs=VOB[:, hl * (8 * 132) + st * 132: hl * (8 * 132) + st * 132 + 129],
                        start=True, stop=True, skip_group_check=True),
                        deps=[kl] + ([up_prev] if up_prev else []))
                    up = B.dve(lambda e, hl=hl, h=h: e.scalar_tensor_tensor(
                        out=RF[:, hl * 132: hl * 132 + 129], in0=RF[:, hl * 132: hl * 132 + 129],
                        scalar=LAM[h] ** 128, in1=PS[2][:, 0:129], op0=ALU.mult, op1=ALU.add),
                        deps=[dr, o2_, rtoks[hl]])
                    rtoks[hl] = B.act(lambda e, hl=hl: e.activation(
                        out=RA[:, hl * 132: hl * 132 + 129], in_=RF[:, hl * 132: hl * 132 + 129], func=AF.Copy), deps=[up])
                    d1 = B.act(lambda e: e.activation(out=TS1[:, 0:1], in_=PS[1][:, 128:129], func=AF.Abs),
                               deps=[o2_] + ([fin_prev] if fin_prev else []))
                    d2 = B.dve(lambda e, h=h, st=st: e.tensor_max(
                        out=TS1[:, 0:1], in0=TS1[:, 0:1], in1=cRR[:, h * 8 + st: h * 8 + st + 1]), deps=[d1])
                    d3 = B.dve(lambda e: e.reciprocal(out=TS1[:, 0:1], in_=TS1[:, 0:1]), deps=[d2])
                    os_ = B.dve(lambda e: e.tensor_scalar_mul(out=TS2[:, 0:128], in0=PS[1][:, 0:128], scalar1=TS1[:, 0:1]),
                                deps=[d3, o2_] + ([fin_prev] if fin_prev else []))
                    sq_ = B.act(lambda e: e.activation(out=TS1[:, 128:256], in_=TS2[:, 0:128], func=AF.Square), deps=[os_])
                    rs_ = B.dve(lambda e: e.reduce_sum(out=TS1[:, 1:2], in_=TS1[:, 128:256], axis=AX), deps=[sq_])
                    rt0 = B.dve(lambda e: e.tensor_scalar(out=TS1[:, 1:2], in0=TS1[:, 1:2], scalar1=1.0 / DH, scalar2=EPS, op0=ALU.mult, op1=ALU.add), deps=[rs_])
                    rt_ = B.act(lambda e: e.activation(out=TS1[:, 1:2], in_=TS1[:, 1:2], func=AF.Sqrt), deps=[rt0])
                    rt_ = B.dve(lambda e: e.reciprocal(out=TS1[:, 1:2], in_=TS1[:, 1:2]), deps=[rt_])
                    on_ = B.dve(lambda e: e.tensor_scalar_mul(out=TS2[:, 128:256], in0=TS2[:, 0:128], scalar1=TS1[:, 1:2]), deps=[rt_])
                    rn_ = B.dve(lambda e, hl=hl, st=st: e.tensor_mul(
                        out=RSN[:], in0=TS2[:, 128:256],
                        in1=SGB[:, st * HG * 128 + hl * 128: st * HG * 128 + hl * 128 + 128]),
                        deps=[on_] + ([fin_prev] if fin_prev else []))
                    tb_ = B.pe(lambda e: e.transpose(out=PSB[:, 128:256], in_=RSN[:], identity=cIB[:]), deps=[rn_])
                    fin_prev = B.dve(lambda e, hl=hl, st=st: e.tensor_copy(
                        out=REST[:, hl * T + st * 128: hl * T + st * 128 + 128], in_=PSB[:, 128:256]), deps=[tb_])
                    am_prev = am
                    os_prev = os_
                    up_prev = up
                    kl_prev = kl
            f_ret = B.fence()
            ocons = {0: None, 1: None}
            op_prev = None
            for eo in range(KT):
                sl = eo % 2
                ldw = B.dma("GP", lambda e, eo=eo, g=g, sl=sl: e.dma_start(
                    out=WS[4 + sl][:, 0:HG * 128].rearrange("p (k m) -> p k m", m=128),
                    in_=ow[g * HG * 128:(g + 1) * HG * 128, eo * 128:(eo + 1) * 128].rearrange("(k p) m -> p k m", p=128)),
                    deps=f_ret + ([ocons[sl]] if ocons[sl] else []))
                for tb in range(2):
                    mmt = None
                    for kk in range(HG):
                        mmt = B.pe(lambda e, kk=kk, tb=tb, sl=sl: e.matmul(
                            out=PS[6][:, 0:512], lhsT=WS[4 + sl][:, kk * 128:(kk + 1) * 128],
                            rhs=REST[:, kk * T + tb * 512: kk * T + tb * 512 + 512],
                            start=(kk == 0), stop=(kk == HG - 1), skip_group_check=True),
                            deps=[ldw] + f_ret + ([op_prev] if op_prev else []), inc=(kk == HG - 1))
                    c0 = eo * T + tb * 512
                    ev = B.dve(lambda e, c0=c0: e.tensor_add(
                        out=XT[:, c0:c0 + 512], in0=PS[6][:, 0:512], in1=XT[:, c0:c0 + 512]),
                        deps=[mmt] + ([x_ret_cons.get(c0)] if x_ret_cons.get(c0) else []))
                    ocons[sl] = ev
                    op_prev = ev
                    x_ret_cons[c0] = ev
        f_retall = B.fence()

        emit_norm("rms", wcol=cfinal, deps=f_retall)
        f_rms2 = B.fence()
        H4 = ar(0, 16 * 512)
        xf_cons = {}
        prev_f2 = []
        for tb in range(2):
            for qr in range(4):
                fccons = {0: None, 1: None}
                h4_toks = []
                gl_prev = None
                for fi in range(KT):
                    fo = qr * KT + fi
                    sl = fi % 2
                    ld1 = B.dma("GP", lambda e, fo=fo, sl=sl: e.dma_start(
                        out=WS[sl][:, 0:KT * 128].rearrange("p (k m) -> p k m", m=128),
                        in_=fc1w[:, fo * 128:(fo + 1) * 128].rearrange("(k p) m -> p k m", p=128)),
                        deps=([fccons[sl]] if fccons[sl] else f_rms2) + prev_f2)
                    mmt = None
                    for k in range(KT):
                        mmt = B.pe(lambda e, k=k, tb=tb, sl=sl, fi=fi: e.matmul(
                            out=PS[0 + (fi % 2)][:, 0:512], lhsT=WS[sl][:, k * 128:(k + 1) * 128],
                            rhs=HT[:, k * T + tb * 512: k * T + tb * 512 + 512],
                            start=(k == 0), stop=(k == KT - 1), skip_group_check=True),
                            deps=[ld1] + f_rms2 + ([gl_prev] if gl_prev else []), inc=(k == KT - 1))
                    g1_ = B.dve(lambda e, fo=fo, fi=fi: e.tensor_scalar_add(
                        out=TS1[:], in0=PS[0 + (fi % 2)][:, 0:512], scalar1=cfc1b[:, fo:fo + 1]),
                        deps=[mmt] + ([gl_prev] if gl_prev else []))
                    g2_ = B.act(lambda e: e.activation(out=TS2[:], in_=TS1[:], func=AF.Square), deps=[g1_])
                    g3_ = B.dve(lambda e: e.tensor_scalar(out=TS2[:], in0=TS2[:], scalar1=0.044715, scalar2=1.0, op0=ALU.mult, op1=ALU.add), deps=[g2_])
                    g4_ = B.dve(lambda e: e.tensor_mul(out=TS2[:], in0=TS2[:], in1=TS1[:]), deps=[g3_])
                    g5_ = B.act(lambda e: e.activation(out=TS2[:], in_=TS2[:], func=AF.Tanh, scale=C_G), deps=[g4_])
                    g6_ = B.dve(lambda e: e.tensor_scalar_add(out=TS2[:], in0=TS2[:], scalar1=1.0), deps=[g5_])
                    g7_ = B.dve(lambda e, fi=fi: e.scalar_tensor_tensor(
                        out=H4[:, fi * 512:(fi + 1) * 512], in0=TS1[:], scalar=0.5, in1=TS2[:],
                        op0=ALU.mult, op1=ALU.mult), deps=[g6_])
                    fccons[sl] = g7_
                    gl_prev = g7_
                    h4_toks.append(g7_)
                f2cons = {0: None, 1: None}
                for eo in range(KT):
                    sl = eo % 2
                    ld2 = B.dma("GP", lambda e, eo=eo, qr=qr, sl=sl: e.dma_start(
                        out=WS[4 + sl][:, 0:KT * 128].rearrange("p (k m) -> p k m", m=128),
                        in_=fc2w[qr * FG:(qr + 1) * FG, eo * 128:(eo + 1) * 128].rearrange("(k p) m -> p k m", p=128)),
                        deps=[f2cons[sl]] if f2cons[sl] else f_rms2)
                    mmt = None
                    for k in range(KT):
                        mmt = B.pe(lambda e, k=k, eo=eo, sl=sl: e.matmul(
                            out=PS[2 + (eo % 2)][:, 0:512], lhsT=WS[4 + sl][:, k * 128:(k + 1) * 128],
                            rhs=H4[:, k * 512:(k + 1) * 512],
                            start=(k == 0), stop=(k == KT - 1), skip_group_check=True),
                            deps=[ld2, h4_toks[k]], inc=(k == KT - 1))
                    c0 = eo * T + tb * 512
                    prevx = xf_cons.get((tb, eo))
                    if qr == 0:
                        ev = B.dve(lambda e, c0=c0, eo=eo: e.scalar_tensor_tensor(
                            out=XT[:, c0:c0 + 512], in0=PS[2 + (eo % 2)][:, 0:512], scalar=cfc2b[:, eo:eo + 1],
                            in1=XT[:, c0:c0 + 512], op0=ALU.add, op1=ALU.add),
                            deps=[mmt] + ([prevx] if prevx else []))
                    else:
                        ev = B.dve(lambda e, c0=c0, eo=eo: e.tensor_add(
                            out=XT[:, c0:c0 + 512], in0=PS[2 + (eo % 2)][:, 0:512], in1=XT[:, c0:c0 + 512]),
                            deps=[mmt] + ([prevx] if prevx else []))
                    f2cons[sl] = ev
                    xf_cons[(tb, eo)] = ev
                prev_f2 = [f2cons[0], f2cons[1]]
        f_ffnb = B.fence()
        rlcons = None
        for kt in range(KT):
            for tb in range(2):
                c0 = kt * T + tb * 512
                rl = B.dma("SQ", lambda e, c0=c0: e.dma_start(out=TS1[:], in_=XSPILL[:, c0:c0 + 512]),
                           deps=f_ffnb + ([rlcons] if rlcons else []))
                rlcons = B.dve(lambda e, c0=c0: e.tensor_add(
                    out=XT[:, c0:c0 + 512], in0=TS1[:], in1=XT[:, c0:c0 + 512]), deps=[rl])
        f_p3 = B.fence()

        f_p4 = emit_golden(f_p3)

        st_toks = []
        allcps = []
        for tt in range(8):
            for q4 in range(4):
                stg = TS1 if (q4 % 2 == 0) else TS2
                cps = []
                for j in range(4):
                    kt = q4 * 4 + j
                    bank = j % 2
                    tr = B.pe(lambda e, kt=kt, tt=tt, bank=bank: e.transpose(
                        out=PS[bank][:, 0:128], in_=XT[:, kt * T + tt * 128: kt * T + tt * 128 + 128], identity=cI[:]),
                        deps=f_p4 + allcps[-2:])
                    cp = B.dve(lambda e, j=j, bank=bank, stg=stg: e.tensor_copy(
                        out=stg[:, j * 128:(j + 1) * 128], in_=PS[bank][:, 0:128]),
                        deps=[tr] + (st_toks[-2:-1] if len(st_toks) >= 2 else []))
                    cps.append(cp)
                    allcps.append(cp)
                st = B.dma("SQ", lambda e, tt=tt, q4=q4, stg=stg: e.dma_start(
                    out=yout[tt * 128:(tt + 1) * 128, q4 * 512:(q4 + 1) * 512], in_=stg[:]), deps=cps)
                st_toks.append(st)
        B.ops.append(("GP", [("SQ%d" % i, B.ctr["SQ%d" % i]) for i in range(8) if B.ctr["SQ%d" % i] > 0],
                      lambda e: e.engine_nop(), []))

        B.emit(block, sems)
    return nc


def _tables(core):
    f8 = np.float64
    out = {}
    eye = np.eye(128)
    out["IDF"] = eye.astype(np.float32)
    out["IDB"] = eye.astype(BF)
    out["ONES1"] = np.ones((1, 128), np.float32)
    out["ONESC"] = np.ones((128, 1), BF)
    P2 = np.zeros((128, 128), f8)
    for i in range(64):
        P2[i, i + 64] = -1.0
        P2[i + 64, i] = 1.0
    out["P2T"] = P2.T.astype(BF)
    PR = np.zeros((128, 128), f8)
    for i in range(64):
        PR[2 * i, 2 * i + 1] = -1.0
        PR[2 * i + 1, 2 * i] = 1.0
    out["PRT"] = PR.T.astype(BF)
    pos = np.arange(core * T, (core + 1) * T, dtype=f8)
    dim = np.arange(64, dtype=f8)
    ang = pos[:, None] * (10000.0 ** (-2.0 * dim / 128.0))[None, :]
    cosf = np.cos(ang); sinf = np.sin(ang)
    cos128 = np.concatenate([cosf, cosf], axis=1).T
    sin128 = np.concatenate([sinf, sinf], axis=1).T
    out["RQC"] = (cos128 * SCALE).astype(BF)
    out["RQS"] = (sin128 * SCALE).astype(BF)
    out["RKC"] = cos128.astype(BF)
    out["RKS"] = sin128.astype(BF)
    a = 1.0 / (10000.0 ** np.linspace(0.0, 1.0, 64))
    a = np.repeat(a, 2)
    angr = pos[None, :] * a[:, None]
    out["TQC"] = np.cos(angr).astype(BF)
    out["TQS"] = np.sin(angr).astype(BF)
    out["TKC"] = (np.cos(angr) * SCALE).astype(BF)
    out["TKS"] = (np.sin(angr) * SCALE).astype(BF)
    lam = np.array(LAM, f8)
    DTm = np.zeros((128, H * 128), f8)
    idx = np.arange(128)
    for h in range(H):
        M = idx[None, :] - idx[:, None]
        DTm[:, h * 128:(h + 1) * 128] = np.where(M >= 0, lam[h] ** np.maximum(M, 0), 0.0)
    out["DTm"] = DTm.astype(BF)
    QL = np.zeros((128, H * 128), f8)
    for h in range(H):
        QL[:, h * 128:(h + 1) * 128] = (lam[h] ** (np.arange(128) + 1.0))[None, :]
    out["QLAM"] = QL.astype(BF)
    LC = np.zeros((128, H), f8)
    for h in range(H):
        LC[:, h] = lam[h] ** (127.0 - np.arange(128))
    out["LCOL"] = LC.astype(np.float32)
    RR = np.zeros((128, H * 8), f8)
    for h in range(H):
        for st in range(8):
            s_glob = core * T + st * 128 + np.arange(128, dtype=f8)
            RR[:, h * 8 + st] = np.sqrt((1.0 - lam[h] ** (s_glob + 1.0)) / (1.0 - lam[h]))
    out["RROW"] = RR.astype(np.float32)
    CD = np.zeros((128, NC * H), f8)
    for cp in range(NC):
        if cp < core:
            for h in range(H):
                CD[:, cp * H + h] = lam[h] ** (1024.0 * (core - 1 - cp))
    out["CDEC"] = CD.astype(np.float32)
    OH = np.zeros((8, NE * 128), np.float32)
    for e_ in range(NE):
        OH[e_, e_ * 128:(e_ + 1) * 128] = 1.0
    out["OH8"] = OH.astype(BF)
    return out


_CACHE = {}


def kernel(**inputs):
    x = np.asarray(inputs["L0_attn_a_x"], np.float32).reshape(T * NC, E)
    if "nc" not in _CACHE:
        _CACHE["nc"] = build_nc()
    nc = _CACHE["nc"]

    def col(v, n):
        return np.ascontiguousarray(np.asarray(v, np.float32).reshape(n, 128).T)

    shared = dict(
        Wq=np.asarray(inputs["L0_attn_a_Wq"], np.float32),
        bq=col(inputs["L0_attn_a_bq"], KT),
        Wk=np.asarray(inputs["L0_attn_a_Wk"], np.float32),
        bk=np.asarray(inputs["L0_attn_a_bk"], np.float32).reshape(128, 1),
        Wv=np.asarray(inputs["L0_attn_a_Wv"], np.float32),
        bv=np.asarray(inputs["L0_attn_a_bv"], np.float32).reshape(1, 128),
        Wo=np.asarray(inputs["L0_attn_a_Wo"], np.float32),
        bo=col(inputs["L0_attn_a_bo"], KT),
        qw=np.asarray(inputs["L0_attn_b_q_w"], np.float32),
        kw=np.asarray(inputs["L0_attn_b_k_w"], np.float32),
        vw=np.asarray(inputs["L0_attn_b_v_w"], np.float32),
        gw=np.asarray(inputs["L0_attn_b_g_w"], np.float32),
        ow=np.asarray(inputs["L0_attn_b_out_w"], np.float32),
        fc1w=np.asarray(inputs["L0_attn_b_fc1_w"], np.float32),
        fc1b=col(inputs["L0_attn_b_fc1_b"], FB // 128),
        fc2w=np.asarray(inputs["L0_attn_b_fc2_w"], np.float32),
        fc2b=col(inputs["L0_attn_b_fc2_b"], KT),
        retnw=col(inputs["L0_attn_b_retn_norm_w"], KT),
        finalw=col(inputs["L0_attn_b_final_norm_w"], KT),
        Wg=np.asarray(inputs["L0_ffn_Wg"], np.float32),
        Wu=np.asarray(inputs["L0_ffn_Wu"], np.float32),
        Wd=np.asarray(inputs["L0_ffn_Wd"], np.float32),
        Wr=np.asarray(inputs["L0_ffn_Wr"], np.float32),
        temp=np.asarray(inputs["L0_ffn_temperature"], np.float32).reshape(1, 1),
    )
    in_maps = []
    for c in range(NC):
        m = dict(shared)
        m["xin"] = np.ascontiguousarray(x[c * T:(c + 1) * T, :])
        m.update(_tables(c))
        in_maps.append(m)
    res = run_bass_kernel_spmd(nc, in_maps, core_ids=list(range(NC)))
    y = np.concatenate([res.results[c]["yout"] for c in range(NC)], axis=0)
    return y.reshape(1, NC * T, E).astype(np.float32)


if __name__ == "__main__":
    import reference
    inp = reference.setup_inputs(0)
    inp = {k: np.asarray(v) for k, v in inp.items()}
    out = kernel(**inp)
    ref = np.asarray(reference.reference(**inp))
    err = np.mean((out - ref) ** 2) / np.mean(ref ** 2)
    print("residual variance ratio:", err)
